# version 01 (final)
# baseline (speedup 1.0000x reference)
import math
import numpy as np
import ml_dtypes

try:
    import jax as _jax
    _jax.config.update("jax_compilation_cache_dir", "/tmp/.trn_jax_cache")
    _jax.config.update("jax_persistent_cache_min_entry_size_bytes", -1)
    _jax.config.update("jax_persistent_cache_min_compile_time_secs", 0)
except Exception:
    pass

import concourse.bass as bass
import concourse.mybir as mybir
from concourse.bass_utils import run_bass_kernel_spmd

F32 = mybir.dt.float32
BF16 = mybir.dt.bfloat16
I32 = mybir.dt.int32
AF = mybir.ActivationFunctionType
ALU = mybir.AluOpType
AX = mybir.AxisListType
BFNP = ml_dtypes.bfloat16

NCORES = 8
T = 2048
D = 1024
FF = 4096
NT = 4096
NTIL = 32
NH = 16
KC = 16384
NLV = 11
BIG = 32768.0
NKCH = 128
SUPS = [12] * 10 + [8]

_CONST = {}


def _v_bits(tt, h):
    out = np.zeros(tt.shape, np.int64)
    x = tt ^ h
    for b in range(NLV):
        out += (x % (1 << (b + 1)) == 0).astype(np.int64)
    return out


def _count_blk(h, tq, ts):
    x = tq[:, None] ^ ts[None, :]
    msb = np.zeros(x.shape, np.int64)
    nz = x > 0
    msb[nz] = np.floor(np.log2(x[nz])).astype(np.int64)
    X0 = np.maximum(0, msb - 1)
    Xmax = np.minimum(np.minimum(_v_bits(tq, h)[:, None], _v_bits(ts, h)[None, :]), NLV - 1)
    return np.maximum(0, Xmax - X0 + 1).astype(np.float32)


def _consts():
    if _CONST:
        return _CONST
    t_all = np.arange(T, dtype=np.int64)
    cmul = np.zeros((NH, 16, 128, 160), np.float32)
    cbias = np.zeros((NH, 16, 128, 160), np.float32)
    for h in range(NH):
        hub = (h % 64) + 64 * np.arange(32, dtype=np.int64)
        for i in range(16):
            tq = t_all[128 * i:128 * (i + 1)]
            cd = _count_blk(h, tq, tq)
            valid = (cd > 0) & (tq[:, None] >= tq[None, :])
            cmul[h, i, :, :128] = np.where(valid, cd, 1.0)
            cbias[h, i, :, :128] = np.where(valid, 0.0, -BIG)
            ch = _count_blk(h, tq, hub)
            vh = (ch > 0) & (hub[None, :] <= tq[:, None]) & \
                 (hub[None, :] // 128 != tq[:, None] // 128)
            cmul[h, i, :, 128:] = np.where(vh, ch, 1.0)
            cbias[h, i, :, 128:] = np.where(vh, 0.0, -BIG)
    _CONST["cmul"] = np.ascontiguousarray(
        cmul.transpose(0, 2, 1, 3).reshape(NH * 128, 16 * 160)).astype(BFNP)
    _CONST["cbias"] = np.ascontiguousarray(
        cbias.transpose(0, 2, 1, 3).reshape(NH * 128, 16 * 160)).astype(BFNP)

    u = np.arange(128)[:, None]
    w = np.arange(128)[None, :]
    _CONST["l0diag"] = np.where(w < u, 0.0, -4.0).astype(np.float32)

    inv = (1.0 / (15000.0 ** (np.arange(0, D, 2, dtype=np.float32) / np.float32(D)))).astype(np.float32)
    fr = np.arange(T, dtype=np.float32)[:, None] * inv[None, :]
    emb = np.concatenate([fr, fr], axis=1)
    _CONST["cosT"] = np.ascontiguousarray(np.cos(emb).T).astype(np.float32)
    _CONST["sinT"] = np.ascontiguousarray(np.sin(emb).T).astype(np.float32)
    _CONST["id128"] = np.eye(128, dtype=BFNP)
    _CONST["ones_row"] = np.ones((1, 128), np.float32)
    return _CONST


class Sem:
    __slots__ = ("name", "n", "h")

    def __init__(self, name):
        self.name, self.n, self.h = name, 0, None


class Prog:
    ENGS = ("sync", "gpsimd", "vector", "scalar", "tensor")

    def __init__(self):
        self.q = {e: [] for e in self.ENGS}
        self.sems = []
        self.bar = self.sem("bar")

    def sem(self, name):
        s = Sem(name)
        self.sems.append(s)
        return s

    def emit(self, eng, fn, inc=()):
        for s, n in inc:
            s.n += n

        def run(e, fn=fn, inc=tuple(inc)):
            ins = fn(e)
            for s, n in inc:
                ins = ins.then_inc(s.h, n)
        self.q[eng].append(run)

    def wait(self, eng, s, val=None):
        v = s.n if val is None else val
        if v > 0:
            self.q[eng].append(lambda e, s=s, v=v: e.wait_ge(s.h, v))

    def barrier(self, dma_sems=()):
        for eng, s in dma_sems:
            self.wait(eng, s)
        for e in self.ENGS:
            self.emit(e, lambda en: en.nop(), inc=[(self.bar, 1)])
        tgt = self.bar.n
        for e in self.ENGS:
            self.wait(e, self.bar, tgt)


def build(nc, taps=(), stop_after=None):
    cst = _consts()
    P = Prog()
    tap_list = []

    dt_in = {}

    def din(name, shape, dtype=F32):
        dt_in[name] = nc.dram_tensor(name, list(shape), dtype, kind="ExternalInput")
        return dt_in[name]

    x_in = din("x", [NT, D])
    din("w1", [FF, D]); din("b1c", [128, 32]); din("b2f", [128, 8])
    din("w2", [D, FF])
    din("wqT", [D, D]); din("wkT", [D, D]); din("wvT", [D, D]); din("woT", [D, D])
    din("bqf", [128, 8]); din("bkf", [128, 8]); din("bvrow", [1, D]); din("borow", [1, D])
    din("wcT", [KC, D])
    din("rmB", [128, 1024], BF16)
    din("ruB", [128, 4096], BF16)
    din("cmul", [NH * 128, 16 * 160], BF16)
    din("cbias", [NH * 128, 16 * 160], BF16)
    din("l0diag", [128, 128])
    din("cosT", [D, T]); din("sinT", [D, T])
    din("id128", [128, 128], BF16)
    din("ones_row", [1, 128])
    din("emaf", [128, 8])
    din("sca", [128, 8])

    y_out = nc.dram_tensor("y", [NT, D], F32, kind="ExternalOutput")

    xn_scr = nc.dram_tensor("xn_scr", [NT, D], F32)
    r_scr = nc.dram_tensor("r_scr", [NT, D], F32)
    h_scr = nc.dram_tensor("h_scr", [FF, NT], F32)
    w1qT_scr = nc.dram_tensor("w1qT_scr", [D, FF], BF16)
    w2qT_scr = nc.dram_tensor("w2qT_scr", [FF, D], BF16)
    row_scr = nc.dram_tensor("row_scr", [128, 16], F32)
    cc1_i = nc.dram_tensor("cc1_i", [128, 1], F32)
    cc1_o = nc.dram_tensor("cc1_o", [128, 1], F32)
    cc2_i = nc.dram_tensor("cc2_i", [128, 1], F32)
    cc2_o = nc.dram_tensor("cc2_o", [128, 1], F32)

    nc.alloc_sbuf_tensor("guard", [128, 4096], mybir.dt.uint8)

    def sb(name, shape, dtype=F32):
        return nc.alloc_sbuf_tensor(name, list(shape), dtype)

    sca = sb("sca_s", [128, 8]); emaf = sb("emaf_s", [128, 8])
    ones = sb("ones_s", [1, 128])
    id128 = sb("id128_s", [128, 128], BF16)
    l0d = sb("l0d_s", [128, 128])
    seqsim = sb("seqsim", [128, 32]); easim = sb("easim", [128, 32])
    gates = sb("gates", [128, 32])
    xmax = sb("xmax", [128, 1]); hmax = sb("hmax", [128, 1])
    sw1c = sb("sw1c", [128, 32]); rsw1c = sb("rsw1c", [128, 32])
    sw2f = sb("sw2f", [128, 8]); rsw2f = sb("rsw2f", [128, 8])
    c2f = sb("c2f", [128, 8]); b2qf = sb("b2qf", [128, 8])
    s1b = sb("s1b", [128, 1]); rs1b = sb("rs1b", [128, 1])
    s2b = sb("s2b", [128, 1]); rs2b = sb("rs2b", [128, 1])
    rowtmp = sb("rowtmp", [1, 128])
    gp_sc = sb("gp_sc", [128, 8])
    Mpair = sb("Mpair", [128, 4096], BF16)
    eman = sb("emanorm", [128, 8], BF16)
    tiny = sb("tiny", [128, 8])
    tiny2 = sb("tiny2", [128, 8])
    e1row = sb("e1row", [1, 16])
    b1q = sb("b1q", [128, 32])
    c1c = sb("c1c", [128, 32])
    b1raw = sb("b1raw", [128, 32])
    qs_i = sb("qs_i", [128, 1], I32)
    qs_g = sb("qs_g", [128, 1]); qs_v = sb("qs_v", [128, 1])
    b2raw = sb("b2raw", [128, 8])
    bqf = sb("bqf_s", [128, 8]); bkf = sb("bkf_s", [128, 8])

    arB = sb("arB", [128, 65536], mybir.dt.uint8)
    arC = sb("arC", [128, 65536], mybir.dt.uint8)
    arD = sb("arD", [128, 65536], mybir.dt.uint8)

    _vc = [0]

    def V(arena, off, shape, dtype):
        _vc[0] += 1
        return nc.alloc_sbuf_tensor_at(
            f"ov{_vc[0]}", list(shape), dtype,
            offset=nc.lookup_mloc(arena).addr + off)

    OnT = V(arB, 0, [128, 8 * NT], BF16)
    xqT = V(arB, 0, [128, 8 * NT], BF16)
    w2qT = V(arB, 0, [128, 32 * 1024], BF16)
    xn2T = V(arB, 0, [128, 8 * NT], BF16)
    w1st = V(arC, 0, [128, 8 * 4096], BF16)
    w2st = V(arC, 0, [128, 32 * 1024], BF16)
    hq8 = V(arC, 0, [128, 32 * 512], BF16)
    Vseq = V(arC, 0, [128, 16 * 1024], BF16)
    aoT = V(arC, 32768, [128, 8 * T], BF16)
    woTC = V(arC, 0, [128, 8 * 1024], BF16)
    raccC = [V(arC, 16384 + 4096 * k, [128, 1024], F32) for k in range(8)]
    wcA = V(arC, 0, [128, 12 * 1024], BF16)
    wcB = V(arC, 24576, [128, 12 * 1024], BF16)
    Gb = V(arC, 49152, [128, 12 * 256], BF16)
    eSb = V(arC, 55296, [128, 2048], F32)
    rmBs = V(arD, 0, [128, 1024], BF16)
    ruBs = V(arD, 2048, [128, 4096], BF16)
    xsl = [V(arD, 0, [128, D], F32), V(arD, 4096, [128, D], F32)]
    xnsl = [V(arD, 8192, [128, D], F32), V(arD, 12288, [128, D], F32)]
    outg = V(arD, 16384, [128, D], F32)
    sqscr = V(arD, 20480, [128, D], BF16)
    onb = V(arD, 22528, [128, D], BF16)
    i32t = V(arD, 24576, [128, D], I32)
    gfx = V(arD, 28672, [128, D], F32)
    wqq = V(arD, 32768, [128, FF], BF16)
    w1ld = [V(arD, 0, [128, D], F32), V(arD, 4096, [128, D], F32)]
    sqscr_f32 = V(arD, 8192, [128, D], F32)
    w2ld = [V(arD, 49152, [128, FF], F32)]
    hcol = [V(arD, 0, [128, FF], F32), V(arD, 16384, [128, FF], F32)]
    w1cb = [V(arD, 36864, [128, 1024], BF16), V(arD, 38912, [128, 1024], BF16)]
    q_v5 = V(arD, 40960, [128, D], F32)
    q_i = V(arD, 45056, [128, D], I32)
    q_g = V(arD, 49152, [128, D], F32)
    rq8 = [V(arD, 4096 * k, [128, 1024], F32) for k in range(4)]
    hld = [V(arD, 16384, [128, 512], F32), V(arD, 18432, [128, 512], F32)]
    m2v5 = V(arD, 20480, [128, 512], F32)
    m2i = V(arD, 22528, [128, 512], I32)
    m2g = V(arD, 24576, [128, 512], F32)
    c2t = V(arD, 26624, [128, D], F32)
    b2qt = V(arD, 30720, [128, D], F32)
    c2row = V(arD, 57344, [1, D], F32)
    b2qrow = V(arD, 61440, [1, D], F32)
    wvTb = V(arD, 0, [128, 8 * 1024], BF16)
    at_q0 = V(arD, 0, [128, T], BF16); at_q4 = V(arD, 4096, [128, T], BF16)
    at_k0 = V(arD, 8192, [128, T], BF16); at_k4 = V(arD, 12288, [128, T], BF16)
    rawA = V(arD, 16384, [128, 1024], F32); rawB = V(arD, 20480, [128, 1024], F32)
    rawC = V(arD, 24576, [128, 1024], F32); rawD = V(arD, 28672, [128, 1024], F32)
    at_cos = V(arD, 32768, [128, 1024], F32); at_sin = V(arD, 36864, [128, 1024], F32)
    at_cos4 = V(arD, 40960, [128, 1024], F32); at_sin4 = V(arD, 45056, [128, 1024], F32)
    at_cm = V(arD, 49152, [128, 2560], BF16)
    at_cb = V(arD, 54272, [128, 2560], BF16)
    at_att = V(arD, 59392, [128, 160], BF16)
    at_at2 = V(arD, 59904, [128, 160], BF16)
    at_dT = V(arD, 60416, [128, 128], BF16)
    at_hT = V(arD, 60672, [32, 128], BF16)
    at_vg = V(arD, 60928, [32, 64], BF16)
    at_nm = V(arD, 61184, [128, 4], F32)
    qkwb = V(arD, 61440, [128, 1024], BF16)
    at_rs = V(arD, 63488, [128, 4], F32)
    racc = [V(arD, 4096 * k, [128, 1024], F32) for k in range(16)]

    PS0 = nc.alloc_psum_tensor("PS0", [128, 2048], F32)
    PS1 = nc.alloc_psum_tensor("PS1", [128, 1024], F32)
    PT = nc.alloc_psum_tensor("PT", [128, 1024], BF16)
    PS2 = nc.alloc_psum_tensor("PS2", [128, 512], F32)

    ld_sy = P.sem("ld_sy"); ld_gp = P.sem("ld_gp"); ld_sc = P.sem("ld_sc")
    c_t = P.sem("c_t"); c_v = P.sem("c_v"); c_s = P.sem("c_s")
    cc = P.sem("cc"); st = P.sem("st")

    def dma(eng, dst, src, slow=False):
        s = {"sync": ld_sy, "gpsimd": ld_gp, "scalar": ld_sc}[eng]
        P.emit(eng, lambda e, d=dst, sr=src, sl=slow:
               e.dma_start(out=d, in_=sr, allow_slow_non_contiguous=True) if sl
               else e.dma_start(out=d, in_=sr), inc=[(s, 16)])
        return s.n

    def dmaT(dst, src):
        P.emit("scalar", lambda e, d=dst, sr=src: e.dma_start_transpose(out=d, in_=sr),
               inc=[(ld_sc, 16)])
        return ld_sc.n

    def dr():
        P.emit("vector", lambda e: e.drain())

    def ew(eng, fn, inc_c=None):
        s = {"vector": c_v, "scalar": c_s, "tensor": c_t}[eng]
        P.emit(eng, fn, inc=[(s, 1)] if inc_c else ())
        if inc_c:
            return s.n

    def tap(name, ap_fn):
        if name not in taps:
            return
        a = ap_fn()
        th = nc.dram_tensor("tap_" + name, list(a.shape), a.dtype, kind="ExternalOutput")
        tap_list.append("tap_" + name)
        P.wait("gpsimd", c_v); P.wait("gpsimd", c_s); P.wait("gpsimd", c_t)
        P.wait("gpsimd", ld_sc); P.wait("gpsimd", ld_sy); P.wait("gpsimd", ld_gp)
        nrow = a.shape[0]
        step = 512 if nrow >= 1024 else nrow
        for r0 in range(0, nrow, step):
            dma("gpsimd", th.ap()[r0:r0 + step], a[r0:r0 + step])
        P.emit("gpsimd", lambda e: e.nop(), inc=[(st, 1)])
        for _e in ("vector", "scalar", "tensor", "sync"):
            P.wait(_e, ld_gp)

    STOPPED = [False]

    def stop(label):
        if stop_after == label:
            STOPPED[0] = True
        return STOPPED[0]

    def rsqrt_chain(src_col, dst_col, eps_add, clamp=None):
        if eps_add:
            dr()
            ew("vector", lambda e: e.tensor_scalar(tiny2[:, 5:6], src_col, eps_add, None, ALU.add), inc_c=1)
            dr()
            v = tiny2[:, 5:6]
        else:
            v = src_col
        P.wait("scalar", c_v)
        ew("scalar", lambda e, v=v: e.activation(tiny2[:, 6:7], v, AF.Sqrt, bias=0.0, scale=1.0), inc_c=1)
        P.wait("vector", c_s)
        ew("vector", lambda e: e.reciprocal(tiny2[:, 7:8], tiny2[:, 6:7]))
        dr()
        ew("vector", lambda e, v=v: e.tensor_tensor(tiny2[:, 7:8], v, tiny2[:, 7:8], ALU.mult))
        dr()
        ew("vector", lambda e: e.tensor_tensor(tiny2[:, 7:8], tiny2[:, 7:8], tiny2[:, 6:7], ALU.add))
        dr()
        if clamp is not None:
            ew("vector", lambda e, c=clamp: e.tensor_scalar(tiny2[:, 7:8], tiny2[:, 7:8], 0.5, c, ALU.mult, ALU.max))
        else:
            ew("vector", lambda e: e.tensor_scalar(tiny2[:, 7:8], tiny2[:, 7:8], 0.5, None, ALU.mult))
        dr()
        ew("vector", lambda e, d=dst_col: e.reciprocal(d, tiny2[:, 7:8]))
        dr()

    def emit_ln(x_ap, xn_ap):
        P.wait("scalar", c_v)
        ew("scalar", lambda e: e.activation(sqscr[:], x_ap, AF.Square, bias=0.0, scale=1.0,
                                            accum_out=tiny2[:, 0:1]), inc_c=1)
        ew("vector", lambda e: e.reduce_sum(tiny2[:, 1:2], x_ap, AX.X))
        dr()
        ew("vector", lambda e: e.tensor_scalar(tiny2[:, 1:2], tiny2[:, 1:2], 1.0 / D, None, ALU.mult))
        P.wait("vector", c_s)
        dr()
        ew("vector", lambda e: e.tensor_tensor(tiny2[:, 2:3], tiny2[:, 1:2], tiny2[:, 1:2], ALU.mult))
        dr()
        ew("vector", lambda e: e.scalar_tensor_tensor(tiny2[:, 2:3], tiny2[:, 0:1], 1.0 / D,
                                                      tiny2[:, 2:3], ALU.mult, ALU.subtract))
        rsqrt_chain(tiny2[:, 2:3], tiny2[:, 3:4], 1e-5)
        ew("vector", lambda e: e.tensor_scalar(xn_ap, x_ap, tiny2[:, 1:2], tiny2[:, 3:4],
                                               ALU.subtract, ALU.mult), inc_c=1)
        dr()

    def emit_round(src_ap, dst_ap, recip_col, relu0, wi, wg, wv5=None):
        dr()
        if relu0:
            ew("vector", lambda e: e.tensor_scalar(wv5, src_ap, recip_col, 0.0, ALU.mult, ALU.max), inc_c=1)
            P.wait("scalar", c_v)
            ew("scalar", lambda e: e.activation(wi, wv5, AF.Identity, bias=sca[:, 3:4], scale=1.0), inc_c=1)
            P.wait("vector", c_s)
            dr()
            ew("vector", lambda e: e.scalar_tensor_tensor(wg, wi, 0.5, wv5, ALU.subtract, ALU.is_gt))
        else:
            ew("vector", lambda e: e.tensor_scalar(wv5, src_ap, recip_col, 0.5, ALU.mult, ALU.add), inc_c=1)
            P.wait("scalar", c_v)
            ew("scalar", lambda e: e.activation(wi, wv5, AF.Identity, bias=0.0, scale=1.0), inc_c=1)
            P.wait("vector", c_s)
            dr()
            ew("vector", lambda e: e.scalar_tensor_tensor(wg, wi, 0.0, wv5, ALU.add, ALU.is_gt))
        dr()
        ew("vector", lambda e: e.tensor_tensor(dst_ap, wi, wg, ALU.subtract), inc_c=1)
        dr()

    dma("sync", sca[:], dt_in["sca"].ap())
    dma("sync", emaf[:], dt_in["emaf"].ap())
    dma("sync", ones[:], dt_in["ones_row"].ap())
    dma("sync", id128[:], dt_in["id128"].ap())
    dma("sync", l0d[:], dt_in["l0diag"].ap())
    dma("sync", rmBs[:], dt_in["rmB"].ap())
    dma("sync", ruBs[:], dt_in["ruB"].ap())
    dma("sync", b1raw[:], dt_in["b1c"].ap())
    dma("sync", b2raw[:], dt_in["b2f"].ap())
    dma("sync", bqf[:], dt_in["bqf"].ap())
    dma("sync", bkf[:], dt_in["bkf"].ap())
    P.wait("vector", ld_sy)
    P.wait("scalar", ld_sy)
    P.wait("tensor", ld_sy)

    for fn in (AF.Exp, AF.Sigmoid, AF.Sqrt, AF.Gelu_apprx_tanh, AF.Gelu,
               AF.Relu, AF.Identity, AF.Square):
        ew("scalar", lambda e, fn=fn: e.activation(tiny2[:, 6:7], sca[:, 7:8], fn,
                                                   bias=0.0, scale=1.0))
    ew("scalar", lambda e: e.activation(tiny[:, 0:1], sca[:, 0:1], AF.Exp, bias=0.0, scale=1.0), inc_c=1)
    P.wait("vector", c_s)
    ew("vector", lambda e: e.tensor_scalar(gp_sc[:, 0:1], tiny[:, 0:1], -1.0, None, ALU.mult))
    ew("scalar", lambda e: e.activation(gp_sc[:, 1:2], sca[:, 1:2], AF.Sigmoid, bias=0.0, scale=1.0))
    ew("scalar", lambda e: e.activation(gp_sc[:, 3:4], sca[:, 2:3], AF.Sigmoid, bias=0.0, scale=1.0), inc_c=1)
    P.wait("vector", c_s)
    ew("vector", lambda e: e.tensor_scalar(gp_sc[:, 2:3], gp_sc[:, 1:2], -1.0, 1.0, ALU.mult, ALU.add))
    ew("vector", lambda e: e.tensor_scalar(gp_sc[:, 4:5], gp_sc[:, 3:4], -1.0, 1.0, ALU.mult, ALU.add))

    ew("vector", lambda e: e.tensor_tensor(tiny2[:, 0:8], emaf[:], emaf[:], ALU.mult))
    dr()
    ew("vector", lambda e: e.reduce_sum(tiny[:, 1:2], tiny2[:, 0:8], AX.X), inc_c=1)
    dr()
    P.wait("tensor", c_v)
    ew("tensor", lambda e: e.matmul(PS2[0:1, 0:1], tiny[:, 1:2], sca[:, 7:8], start=True, stop=True), inc_c=1)
    P.wait("vector", c_t)
    ew("vector", lambda e: e.tensor_copy(e1row[:, 0:1], PS2[0:1, 0:1]), inc_c=1)
    P.wait("scalar", c_v)
    ew("scalar", lambda e: e.activation(e1row[:, 1:2], e1row[:, 0:1], AF.Sqrt, bias=0.0, scale=1.0), inc_c=1)
    P.wait("vector", c_s)
    ew("vector", lambda e: e.reciprocal(e1row[:, 2:3], e1row[:, 1:2]))
    dr()
    ew("vector", lambda e: e.tensor_tensor(e1row[:, 3:4], e1row[:, 0:1], e1row[:, 2:3], ALU.mult))
    dr()
    ew("vector", lambda e: e.tensor_tensor(e1row[:, 3:4], e1row[:, 3:4], e1row[:, 1:2], ALU.add))
    dr()
    ew("vector", lambda e: e.tensor_scalar(e1row[:, 3:4], e1row[:, 3:4], 0.5, 1e-12, ALU.mult, ALU.max))
    dr()
    ew("vector", lambda e: e.reciprocal(e1row[:, 4:5], e1row[:, 3:4]), inc_c=1)
    P.wait("tensor", c_v)
    ew("tensor", lambda e: e.matmul(PS2[:, 1:2], ones[:], e1row[:, 4:5], start=True, stop=True), inc_c=1)
    P.wait("vector", c_t)
    ew("vector", lambda e: e.tensor_copy(tiny[:, 2:3], PS2[:, 1:2]))
    dr()
    ew("vector", lambda e: e.tensor_scalar(eman[:], emaf[:], tiny[:, 2:3], None, ALU.mult), inc_c=1)

    for p4 in range(32):
        for pi in range(4):
            p = 4 * p4 + pi
            rb = 32 * (p % 4)
            ew("tensor", lambda e, rb=rb, p4=p4: e.matmul(
                PS0[rb:rb + 32, 128 * (rb // 32):128 * (rb // 32) + 128],
                rmBs[rb:rb + 32, 32 * p4:32 * p4 + 32],
                ruBs[rb:rb + 32, 128 * p4:128 * p4 + 128],
                start=True, stop=True, tile_position=(rb, rb)), inc_c=1)
        P.wait("vector", c_t)
        for pi in range(4):
            rb = 32 * pi
            ew("vector", lambda e, rb=rb, p4=p4: e.tensor_copy(
                Mpair[rb:rb + 32, 128 * p4:128 * p4 + 128],
                PS0[rb:rb + 32, 128 * (rb // 32):128 * (rb // 32) + 128]), inc_c=1)
        P.wait("tensor", c_v)
    P.barrier(dma_sems=[("sync", ld_sy), ("scalar", ld_sc), ("gpsimd", ld_gp)])
    if stop("ph0"):
        return dt_in, y_out, tap_list, P

    ew("vector", lambda e: e.memset(xmax[:], -1e30), inc_c=1)
    xfree = {}
    xnfree = {}
    for i in range(NTIL):
        sl = i % 2
        if ("x", sl) in xfree:
            P.wait("sync", c_v, xfree[("x", sl)])
        nld = dma("sync", xsl[sl][:], x_in.ap()[128 * i:128 * (i + 1), :])
        if ("xn", sl) in xnfree:
            P.wait("scalar", ld_gp, xnfree[("xn", sl)])
            P.wait("vector", ld_gp, xnfree[("xn", sl)])
        P.wait("scalar", ld_sy, nld)
        P.wait("vector", ld_sy, nld)
        emit_ln(xsl[sl][:], xnsl[sl][:])
        xfree[("x", sl)] = c_v.n
        P.wait("gpsimd", c_v)
        xnw = dma("gpsimd", xn_scr.ap()[128 * i:128 * (i + 1), :], xnsl[sl][:])
        xnfree[("xn", sl)] = xnw
        ew("vector", lambda e, sl=sl: e.reduce_max(tiny[:, 3:4], xnsl[sl][:], AX.X))
        dr()
        ew("vector", lambda e: e.tensor_tensor(xmax[:], xmax[:], tiny[:, 3:4], ALU.max))
        P.wait("scalar", c_v)
        ew("scalar", lambda e, sl=sl: e.activation(outg[:], xnsl[sl][:], AF.Gelu_apprx_tanh,
                                                   bias=0.0, scale=1.0), inc_c=1)
        ew("scalar", lambda e: e.activation(sqscr[:], outg[:], AF.Square, bias=0.0, scale=1.0,
                                            accum_out=tiny[:, 4:5]), inc_c=1)
        P.wait("vector", c_s)
        rsqrt_chain(tiny[:, 4:5], tiny[:, 5:6], None, clamp=1e-12)
        prevT = ld_sc.n
        ew("vector", lambda e: e.tensor_scalar(onb[:], outg[:], tiny[:, 5:6], None, ALU.mult), inc_c=1)
        P.wait("scalar", c_v)
        for cch in range(8):
            dmaT(OnT[:, cch * NT + 128 * i: cch * NT + 128 * (i + 1)],
                 onb[:, 128 * cch:128 * (cch + 1)])
        P.wait("vector", ld_sc)
    P.wait("gpsimd", c_v)
    dma("gpsimd", cc1_i.ap(), xmax[:])
    P.wait("gpsimd", ld_gp)
    P.emit("gpsimd", lambda e: e.collective_compute(
        "AllReduce", ALU.max, replica_groups=[list(range(NCORES))],
        ins=[cc1_i.ap().opt()], outs=[cc1_o.ap().opt()]), inc=[(cc, 1)])
    P.wait("gpsimd", cc)
    dma("gpsimd", rowtmp[:], cc1_o.ap().rearrange("p one -> (one p)").unsqueeze(0))
    tap("xn", lambda: xn_scr.ap())
    P.barrier(dma_sems=[("sync", ld_sy), ("scalar", ld_sc), ("gpsimd", ld_gp)])
    if stop("ph2"):
        return dt_in, y_out, tap_list, P

    P.wait("vector", ld_gp)
    ew("vector", lambda e: e.reduce_max(e1row[:, 5:6], rowtmp[:], AX.X))
    dr()
    ew("vector", lambda e: e.tensor_scalar(e1row[:, 5:6], e1row[:, 5:6], 1e-8, 1.0 / 255.0,
                                           ALU.max, ALU.mult), inc_c=1)
    P.wait("tensor", c_v)
    ew("tensor", lambda e: e.matmul(PS2[:, 2:3], ones[:], e1row[:, 5:6], start=True, stop=True), inc_c=1)
    P.wait("vector", c_t)
    ew("vector", lambda e: e.tensor_copy(s1b[:], PS2[:, 2:3]))
    dr()
    ew("vector", lambda e: e.reciprocal(rs1b[:], s1b[:]), inc_c=1)

    ps_ev = {0: 0, 512: 0}
    ema_ev = [0]

    def emit_sim(s, i):
        g = 16 * s + i
        for j in range(i // 4 + 1):
            slot = (0, 512)[(i + j) % 2]
            P.wait("tensor", c_v, ps_ev[slot])
            for cch in range(8):
                P.emit("tensor", lambda e, cch=cch, slot=slot, s=s, i=i, j=j: e.matmul(
                    PS0[:, slot:slot + 512],
                    OnT[:, cch * NT + s * T + 128 * i: cch * NT + s * T + 128 * i + 128],
                    OnT[:, cch * NT + s * T + 512 * j: cch * NT + s * T + 512 * j + 512],
                    start=(cch == 0), stop=(cch == 7)),
                    inc=[(c_t, 1)] if cch == 7 else ())
            P.wait("vector", c_t)
            if j == i // 4:
                k = i % 4
                ew("vector", lambda e, slot=slot, k=k: e.tensor_tensor(
                    gfx[:, 0:128], PS0[:, slot + 128 * k:slot + 128 * k + 128], l0d[:], ALU.add))
                dr()
                ew("vector", lambda e: e.reduce_max(tiny[:, 6:7], gfx[:, 0:128], AX.X))
                dr()
                if k > 0:
                    ew("vector", lambda e, slot=slot, k=k: e.reduce_max(
                        tiny[:, 7:8], PS0[:, slot:slot + 128 * k], AX.X))
                    dr()
                    ew("vector", lambda e: e.tensor_tensor(
                        tiny[:, 6:7], tiny[:, 6:7], tiny[:, 7:8], ALU.max))
                    dr()
                if j == 0:
                    ew("vector", lambda e, g=g: e.tensor_copy(seqsim[:, g:g + 1], tiny[:, 6:7]), inc_c=1)
                else:
                    ew("vector", lambda e, g=g: e.tensor_tensor(
                        seqsim[:, g:g + 1], seqsim[:, g:g + 1], tiny[:, 6:7], ALU.max), inc_c=1)
            elif j == 0:
                ew("vector", lambda e, slot=slot, g=g: e.reduce_max(
                    seqsim[:, g:g + 1], PS0[:, slot:slot + 512], AX.X), inc_c=1)
            else:
                ew("vector", lambda e, slot=slot: e.reduce_max(
                    tiny[:, 6:7], PS0[:, slot:slot + 512], AX.X))
                dr()
                ew("vector", lambda e, g=g: e.tensor_tensor(
                    seqsim[:, g:g + 1], seqsim[:, g:g + 1], tiny[:, 6:7], ALU.max), inc_c=1)
            ps_ev[slot] = c_v.n
        P.wait("tensor", c_v, ema_ev[0])
        for cch in range(8):
            P.emit("tensor", lambda e, cch=cch, g=g: e.matmul(
                PS2[:, 3:4], OnT[:, cch * NT + 128 * g: cch * NT + 128 * g + 128],
                eman[:, cch:cch + 1], start=(cch == 0), stop=(cch == 7)),
                inc=[(c_t, 1)] if cch == 7 else ())
        P.wait("vector", c_t)
        ew("vector", lambda e, g=g: e.tensor_copy(easim[:, g:g + 1], PS2[:, 3:4]), inc_c=1)
        ema_ev[0] = c_v.n

    wfree = {}

    def emit_w1(j):
        sl = j % 2
        if ("w1", sl) in wfree:
            P.wait("sync", c_v, wfree[("w1", sl)])
        nld = dma("sync", w1ld[sl][:], dt_in["w1"].ap()[128 * j:128 * (j + 1), :])
        P.wait("vector", ld_sy, nld)
        ew("vector", lambda e, sl=sl, j=j: e.tensor_reduce(
            sw1c[:, j:j + 1], w1ld[sl][:], AX.X, ALU.max, apply_absolute_value=True))
        dr()
        ew("vector", lambda e, j=j: e.tensor_scalar(sw1c[:, j:j + 1], sw1c[:, j:j + 1],
                                                    1.0 / 127.0, None, ALU.mult))
        dr()
        ew("vector", lambda e, j=j: e.reciprocal(rsw1c[:, j:j + 1], sw1c[:, j:j + 1]), inc_c=1)
        P.wait("scalar", ld_sc)
        P.wait("vector", ld_sc)
        emit_round(w1ld[sl][:], wqq[:, 0:1024], rsw1c[:, j:j + 1], False, i32t[:], gfx[:], sqscr_f32[:])
        wfree[("w1", sl)] = c_v.n
        P.wait("scalar", c_v)
        for cch in range(8):
            dmaT(w1st[:, cch * 4096 + 128 * j: cch * 4096 + 128 * (j + 1)],
                 wqq[:, 128 * cch:128 * (cch + 1)])

    def emit_w2(j):
        if ("w2", 0) in wfree:
            P.wait("sync", c_v, wfree[("w2", 0)])
        nld = dma("sync", w2ld[0][:], dt_in["w2"].ap()[128 * j:128 * (j + 1), :])
        P.wait("vector", ld_sy, nld)
        ew("vector", lambda e, j=j: e.tensor_reduce(
            sw2f[:, j:j + 1], w2ld[0][:], AX.X, ALU.max, apply_absolute_value=True))
        dr()
        ew("vector", lambda e, j=j: e.tensor_scalar(sw2f[:, j:j + 1], sw2f[:, j:j + 1],
                                                    1.0 / 127.0, None, ALU.mult))
        dr()
        ew("vector", lambda e, j=j: e.reciprocal(rsw2f[:, j:j + 1], sw2f[:, j:j + 1]), inc_c=1)
        P.wait("scalar", ld_sc)
        P.wait("vector", ld_sc)
        for pc in range(4):
            emit_round(w2ld[0][:, 1024 * pc:1024 * (pc + 1)], wqq[:, 1024 * pc:1024 * (pc + 1)],
                       rsw2f[:, j:j + 1], False, i32t[:], gfx[:], sqscr_f32[:])
        wfree[("w2", 0)] = c_v.n
        P.wait("scalar", c_v)
        for fc in range(32):
            dmaT(w2st[:, 1024 * fc + 128 * j: 1024 * fc + 128 * (j + 1)],
                 wqq[:, 128 * fc:128 * (fc + 1)])

    sim_units = [(s, i) for s in range(2) for i in range(16)]
    w_units = [("w1", j) for j in range(32)] + [("flush1", 0)] + [("w2", j) for j in range(8)]
    wi_ = 0

    def emit_flush1():
        P.wait("gpsimd", ld_sc)
        P.wait("gpsimd", c_v)
        for cch in range(8):
            dma("gpsimd", w1qT_scr.ap()[128 * cch:128 * (cch + 1), :],
                w1st[:, 4096 * cch:4096 * (cch + 1)])
        P.wait("scalar", ld_gp)

    for k, (s, i) in enumerate(sim_units):
        emit_sim(s, i)
        while wi_ * len(sim_units) < (k + 1) * len(w_units):
            kind, j = w_units[wi_]
            if kind == "w1":
                emit_w1(j)
            elif kind == "flush1":
                emit_flush1()
            else:
                emit_w2(j)
            wi_ += 1
    P.wait("gpsimd", ld_sc)
    P.wait("gpsimd", c_v)
    for fc in range(32):
        dma("gpsimd", w2qT_scr.ap()[128 * fc:128 * (fc + 1), :], w2st[:, 1024 * fc:1024 * (fc + 1)])

    ew("vector", lambda e: e.tensor_copy(seqsim[0:1, 0:1], easim[0:1, 0:1]))
    ew("vector", lambda e: e.tensor_copy(seqsim[0:1, 16:17], easim[0:1, 16:17]))
    ew("vector", lambda e: e.tensor_scalar(gates[:], easim[:], gp_sc[:, 4:5], None, ALU.mult))
    dr()
    ew("vector", lambda e: e.scalar_tensor_tensor(gates[:], seqsim[:], gp_sc[:, 3:4], gates[:],
                                                  ALU.mult, ALU.add), inc_c=1)
    P.wait("scalar", c_v)
    ew("scalar", lambda e: e.activation(gates[:], gates[:], AF.Exp, bias=0.0, scale=gp_sc[:, 0:1]), inc_c=1)
    P.wait("vector", c_s)
    ew("vector", lambda e: e.tensor_scalar(gates[:], gates[:], gp_sc[:, 1:2], gp_sc[:, 2:3],
                                           ALU.mult, ALU.add), inc_c=1)
    tap("gates", lambda: gates[:])
    tap("seqsim", lambda: seqsim[:])
    tap("easim", lambda: easim[:])
    tap("s1b", lambda: s1b[:])
    P.barrier(dma_sems=[("sync", ld_sy), ("scalar", ld_sc), ("gpsimd", ld_gp)])
    if stop("ph3"):
        return dt_in, y_out, tap_list, P

    for i in range(NTIL):
        sl = i % 2
        if i >= 2:
            P.wait("sync", c_v)
            P.wait("gpsimd", c_v)
        nx = dma("sync", xsl[sl][:], x_in.ap()[128 * i:128 * (i + 1), :])
        nn = dma("gpsimd", xnsl[sl][:], xn_scr.ap()[128 * i:128 * (i + 1), :])
        P.wait("scalar", ld_sy, nx)
        P.wait("scalar", ld_gp, nn)
        P.wait("vector", ld_sy, nx)
        P.wait("vector", ld_gp, nn)
        ew("scalar", lambda e, sl=sl: e.activation(outg[:], xnsl[sl][:], AF.Gelu_apprx_tanh,
                                                   bias=0.0, scale=1.0), inc_c=1)
        P.wait("vector", c_s)
        ew("vector", lambda e, i=i: e.tensor_scalar(outg[:], outg[:], gates[:, i:i + 1], None, ALU.mult))
        dr()
        ew("vector", lambda e, sl=sl: e.tensor_tensor(xsl[sl][:], xsl[sl][:], outg[:], ALU.add), inc_c=1)
        P.wait("gpsimd", c_v)
        dma("gpsimd", r_scr.ap()[128 * i:128 * (i + 1), :], xsl[sl][:])
        P.wait("vector", ld_sc)
        P.wait("scalar", ld_sc)
        emit_round(xnsl[sl][:], wqq[:, 0:1024], rs1b[:], True, i32t[:], gfx[:], q_v5[:])
        P.wait("scalar", c_v)
        for cch in range(8):
            dmaT(xqT[:, cch * NT + 128 * i: cch * NT + 128 * (i + 1)],
                 wqq[:, 128 * cch:128 * (cch + 1)])
    tap("r1", lambda: r_scr.ap())
    tap("xqT", lambda: xqT[:])
    tap("w1qT", lambda: w1qT_scr.ap())
    P.barrier(dma_sems=[("sync", ld_sy), ("scalar", ld_sc), ("gpsimd", ld_gp)])
    if stop("ph6"):
        return dt_in, y_out, tap_list, P

    ew("vector", lambda e: e.memset(hmax[:], 0.0), inc_c=1)
    w1c_free = {}
    hcol_free = {}
    for fc in range(32):
        sl = fc % 2
        if ("w1c", sl) in w1c_free:
            P.wait("sync", c_t, w1c_free[("w1c", sl)])
        for dc in range(8):
            dma("sync", w1cb[sl][:, 128 * dc:128 * (dc + 1)],
                w1qT_scr.ap()[128 * dc:128 * (dc + 1), 128 * fc:128 * (fc + 1)])
        nld = ld_sy.n
        ew("vector", lambda e, fc=fc: e.tensor_scalar(c1c[:, fc:fc + 1], sw1c[:, fc:fc + 1],
                                                      s1b[:, 0:1], None, ALU.mult))
        dr()
        ew("vector", lambda e, fc=fc: e.reciprocal(tiny[:, 7:8], c1c[:, fc:fc + 1]), inc_c=1)
        emit_round(b1raw[:, fc:fc + 1], tiny2[:, 4:5], tiny[:, 7:8], False,
                   qs_i[:], qs_g[:], qs_v[:])
        ew("vector", lambda e, fc=fc: e.tensor_tensor(b1q[:, fc:fc + 1], tiny2[:, 4:5],
                                                      c1c[:, fc:fc + 1], ALU.mult), inc_c=1)
        dr()
        hsl = fc % 2
        if ("hcol", hsl) in hcol_free:
            P.wait("scalar", ld_gp, hcol_free[("hcol", hsl)])
        for tc in range(8):
            slot = (0, 512)[tc % 2]
            P.wait("tensor", c_s, ps_ev.get(("m1", slot), 0))
            P.wait("tensor", ld_sy, nld)
            for dc in range(8):
                P.emit("tensor", lambda e, dc=dc, sl=sl, tc=tc, slot=slot: e.matmul(
                    PS0[:, slot:slot + 512],
                    w1cb[sl][:, 128 * dc:128 * (dc + 1)],
                    xqT[:, dc * NT + 512 * tc: dc * NT + 512 * (tc + 1)],
                    start=(dc == 0), stop=(dc == 7)),
                    inc=[(c_t, 1)] if dc == 7 else ())
            P.wait("scalar", c_t)
            P.wait("scalar", c_v)
            ew("scalar", lambda e, slot=slot, hsl=hsl, tc=tc, fc=fc: e.activation(
                hcol[hsl][:, 512 * tc:512 * (tc + 1)], PS0[:, slot:slot + 512],
                AF.Relu, bias=b1q[:, fc:fc + 1], scale=c1c[:, fc:fc + 1]), inc_c=1)
            ps_ev[("m1", slot)] = c_s.n
        w1c_free[("w1c", sl)] = c_t.n
        P.wait("vector", c_s)
        ew("vector", lambda e, hsl=hsl: e.reduce_max(tiny[:, 6:7], hcol[hsl][:], AX.X))
        dr()
        ew("vector", lambda e: e.tensor_tensor(hmax[:], hmax[:], tiny[:, 6:7], ALU.max), inc_c=1)
        P.wait("gpsimd", c_s)
        hcol_free[("hcol", hsl)] = dma("gpsimd", h_scr.ap()[128 * fc:128 * (fc + 1), :], hcol[hsl][:])
    P.wait("gpsimd", c_v)
    dma("gpsimd", cc2_i.ap(), hmax[:])
    P.wait("gpsimd", ld_gp)
    P.emit("gpsimd", lambda e: e.collective_compute(
        "AllReduce", ALU.max, replica_groups=[list(range(NCORES))],
        ins=[cc2_i.ap().opt()], outs=[cc2_o.ap().opt()]), inc=[(cc, 1)])
    P.wait("gpsimd", cc)
    dma("gpsimd", rowtmp[:], cc2_o.ap().rearrange("p one -> (one p)").unsqueeze(0))
    tap("h0", lambda: h_scr.ap()[0:128, :])
    tap("hfull", lambda: h_scr.ap())
    P.barrier(dma_sems=[("sync", ld_sy), ("scalar", ld_sc), ("gpsimd", ld_gp)])
    if stop("ph7"):
        return dt_in, y_out, tap_list, P

    P.wait("vector", ld_gp)
    ew("vector", lambda e: e.reduce_max(e1row[:, 6:7], rowtmp[:], AX.X))
    dr()
    ew("vector", lambda e: e.tensor_scalar(e1row[:, 6:7], e1row[:, 6:7], 1e-8, 1.0 / 255.0,
                                           ALU.max, ALU.mult), inc_c=1)
    P.wait("tensor", c_v)
    ew("tensor", lambda e: e.matmul(PS2[:, 2:3], ones[:], e1row[:, 6:7], start=True, stop=True), inc_c=1)
    P.wait("vector", c_t)
    ew("vector", lambda e: e.tensor_copy(s2b[:], PS2[:, 2:3]))
    dr()
    ew("vector", lambda e: e.reciprocal(rs2b[:], s2b[:]), inc_c=1)
    ew("vector", lambda e: e.tensor_scalar(c2f[:], sw2f[:], s2b[:, 0:1], None, ALU.mult), inc_c=1)
    dr()
    for jc in range(8):
        ew("vector", lambda e, jc=jc: e.reciprocal(tiny[:, 7:8], c2f[:, jc:jc + 1]), inc_c=1)
        emit_round(b2raw[:, jc:jc + 1], tiny2[:, 4:5], tiny[:, 7:8], False,
                   qs_i[:], qs_g[:], qs_v[:])
        ew("vector", lambda e, jc=jc: e.tensor_tensor(b2qf[:, jc:jc + 1], tiny2[:, 4:5],
                                                      c2f[:, jc:jc + 1], ALU.mult), inc_c=1)
        dr()
    P.wait("gpsimd", c_v)
    dma("gpsimd", row_scr.ap()[:, 0:8], c2f[:])
    dma("gpsimd", row_scr.ap()[:, 8:16], b2qf[:])
    P.wait("gpsimd", ld_gp)
    dma("gpsimd", c2row[:], row_scr.ap()[:, 0:8].transpose([1, 0]).unsqueeze(0), slow=True)
    dma("gpsimd", b2qrow[:], row_scr.ap()[:, 8:16].transpose([1, 0]).unsqueeze(0), slow=True)
    P.wait("tensor", ld_gp)
    for hf in range(2):
        ew("tensor", lambda e, hf=hf: e.matmul(PS0[:, 512 * hf:512 * (hf + 1)], ones[:],
                                               c2row[:, 512 * hf:512 * (hf + 1)],
                                               start=True, stop=True), inc_c=1)
        ew("tensor", lambda e, hf=hf: e.matmul(PS0[:, 1024 + 512 * hf:1024 + 512 * (hf + 1)], ones[:],
                                               b2qrow[:, 512 * hf:512 * (hf + 1)],
                                               start=True, stop=True), inc_c=1)
    P.wait("vector", c_t)
    ew("vector", lambda e: e.tensor_copy(c2t[:], PS0[:, 0:1024]))
    ew("vector", lambda e: e.tensor_copy(b2qt[:], PS0[:, 1024:2048]), inc_c=1)
    tap("s2b", lambda: s2b[:])
    tap("c2t", lambda: c2t[:])
    tap("b2qt", lambda: b2qt[:])
    tap("w2qTs", lambda: w2qT_scr.ap())
    P.barrier(dma_sems=[("sync", ld_sy), ("scalar", ld_sc), ("gpsimd", ld_gp)])
    if stop("ph8"):
        return dt_in, y_out, tap_list, P

    for fc in range(32):
        dma("sync", w2qT[:, 1024 * fc:1024 * (fc + 1)], w2qT_scr.ap()[128 * fc:128 * (fc + 1), :])
    nw2 = ld_sy.n
    P.wait("tensor", ld_sy, nw2)
    for q8 in range(8):
        t0 = 512 * q8
        P.wait("sync", ld_gp)
        for k in range(4):
            dma("sync", rq8[k][:], r_scr.ap()[t0 + 128 * k:t0 + 128 * (k + 1), :])
        nrq = ld_sy.n
        P.wait("vector", c_t)
        for fc in range(32):
            hs = fc % 2
            P.wait("gpsimd", c_v)
            nh = dma("gpsimd", hld[hs][:], h_scr.ap()[128 * fc:128 * (fc + 1), t0:t0 + 512])
            P.wait("vector", ld_gp, nh)
            P.wait("scalar", ld_gp, nh)
            emit_round(hld[hs][:], hq8[:, 512 * fc:512 * (fc + 1)], rs2b[:], False,
                       m2i[:], m2g[:], m2v5[:])
        nqv = c_v.n
        for tt in range(4):
            slot = (0, 1024)[tt % 2]
            P.wait("tensor", c_v, ps_ev.get(("m2", slot), 0))
            P.wait("tensor", c_v, nqv)
            P.wait("tensor", ld_sy, nrq)
            for fc in range(32):
                for hf in range(2):
                    P.emit("tensor", lambda e, fc=fc, hf=hf, tt=tt, slot=slot: e.matmul(
                        PS0[:, slot + 512 * hf: slot + 512 * (hf + 1)],
                        hq8[:, 512 * fc + 128 * tt: 512 * fc + 128 * (tt + 1)],
                        w2qT[:, 1024 * fc + 512 * hf: 1024 * fc + 512 * (hf + 1)],
                        start=(fc == 0), stop=(fc == 31)),
                        inc=[(c_t, 1)] if (fc == 31 and hf == 1) else ())
            P.wait("vector", c_t)
            P.wait("vector", ld_sy, nrq)
            ew("vector", lambda e, slot=slot: e.tensor_tensor(
                q_v5[:], PS0[:, slot:slot + 1024], c2t[:], ALU.mult))
            ew("vector", lambda e: e.tensor_tensor(q_v5[:], q_v5[:], b2qt[:], ALU.add))
            ew("vector", lambda e, tt=tt: e.tensor_tensor(rq8[tt][:], rq8[tt][:], q_v5[:], ALU.add), inc_c=1)
            ps_ev[("m2", slot)] = c_v.n
        if q8 == 4:
            tap("hq4", lambda: hq8[:])
            tap("rqa4", lambda: rq8[0][:])
        P.wait("gpsimd", c_v)
        for k in range(4):
            dma("gpsimd", r_scr.ap()[t0 + 128 * k:t0 + 128 * (k + 1), :], rq8[k][:])
    tap("r2", lambda: r_scr.ap())
    tap("hq8", lambda: hq8[:])
    P.barrier(dma_sems=[("sync", ld_sy), ("scalar", ld_sc), ("gpsimd", ld_gp)])
    if stop("ph85"):
        return dt_in, y_out, tap_list, P

    for i in range(NTIL):
        sl = i % 2
        if i >= 2:
            P.wait("sync", c_v)
        nx = dma("sync", xsl[sl][:], r_scr.ap()[128 * i:128 * (i + 1), :])
        P.wait("scalar", ld_sy, nx)
        P.wait("vector", ld_sy, nx)
        P.wait("vector", ld_sc)
        emit_ln(xsl[sl][:], xnsl[sl][:])
        ew("vector", lambda e, sl=sl: e.tensor_copy(onb[:], xnsl[sl][:]), inc_c=1)
        dr()
        P.wait("scalar", c_v)
        for cch in range(8):
            dmaT(xn2T[:, cch * NT + 128 * i: cch * NT + 128 * (i + 1)],
                 onb[:, 128 * cch:128 * (cch + 1)])
    P.barrier(dma_sems=[("sync", ld_sy), ("scalar", ld_sc), ("gpsimd", ld_gp)])
    if stop("ph9"):
        return dt_in, y_out, tap_list, P

    rope_t1 = V(arD, 49152, [128, 1024], F32)
    rope_t2 = V(arD, 53248, [128, 1024], F32)
    bvr = V(arD, 16384, [1, D], F32)
    at_ms = V(arD, 20480, [128, 160], F32)
    bor = V(arC, 16384, [1, D], F32)

    def attn_seq(s):
        for dc in range(8):
            dma("gpsimd", wvTb[:, 1024 * dc:1024 * (dc + 1)],
                dt_in["wvT"].ap()[128 * dc:128 * (dc + 1), :])
        nwv = ld_gp.n
        nbv = dma("sync", bvr[:], dt_in["bvrow"].ap())
        P.wait("tensor", ld_gp, nwv)
        P.wait("tensor", ld_sy, nbv)
        for tt in range(16):
            for eh in range(2):
                slot = (0, 512)[eh]
                P.wait("tensor", c_v, ps_ev.get(("vp", slot), 0))
                for dc in range(8):
                    P.emit("tensor", lambda e, dc=dc, eh=eh, tt=tt, slot=slot, s=s: e.matmul(
                        PS0[:, slot:slot + 512],
                        xn2T[:, dc * NT + s * T + 128 * tt: dc * NT + s * T + 128 * (tt + 1)],
                        wvTb[:, 1024 * dc + 512 * eh: 1024 * dc + 512 * (eh + 1)],
                        start=(dc == 0), stop=False), inc=())
                P.emit("tensor", lambda e, eh=eh, slot=slot: e.matmul(
                    PS0[:, slot:slot + 512], ones[:], bvr[:, 512 * eh:512 * (eh + 1)],
                    start=False, stop=True), inc=[(c_t, 1)])
                P.wait("vector", c_t)
                ew("vector", lambda e, tt=tt, eh=eh, slot=slot: e.tensor_copy(
                    Vseq[:, 1024 * tt + 512 * eh: 1024 * tt + 512 * (eh + 1)],
                    PS0[:, slot:slot + 512]), inc_c=1)
                ps_ev[("vp", slot)] = c_v.n
        ps_ev[("qk", 0)] = max(ps_ev.get(("qk", 0), 0), ps_ev.get(("vp", 0), 0))
        ps_ev[("qk", 512)] = max(ps_ev.get(("qk", 512), 0), ps_ev.get(("vp", 512), 0))

        for pg in range(4):
            for hh in range(2):
                tb = 1024 * hh
                P.wait("sync", c_v)
                dma("sync", at_cos[:], dt_in["cosT"].ap()[128 * pg:128 * (pg + 1), tb:tb + 1024])
                dma("sync", at_sin[:], dt_in["sinT"].ap()[128 * pg:128 * (pg + 1), tb:tb + 1024])
                dma("sync", at_cos4[:], dt_in["cosT"].ap()[128 * (pg + 4):128 * (pg + 5), tb:tb + 1024])
                nct = dma("sync", at_sin4[:], dt_in["sinT"].ap()[128 * (pg + 4):128 * (pg + 5), tb:tb + 1024])
                for (wname, bia, cidx, rawt) in (("wqT", bqf, pg, rawA), ("wqT", bqf, pg + 4, rawB),
                                                 ("wkT", bkf, pg, rawC), ("wkT", bkf, pg + 4, rawD)):
                    P.wait("gpsimd", c_t, ps_ev.get("qkw", 0))
                    for dc in range(8):
                        dma("gpsimd", qkwb[:, 128 * dc:128 * (dc + 1)],
                            dt_in[wname].ap()[128 * dc:128 * (dc + 1), 128 * cidx:128 * (cidx + 1)])
                    nqk = ld_gp.n
                    P.wait("tensor", ld_gp, nqk)
                    for tc in range(2):
                        slot = (0, 512)[tc]
                        P.wait("tensor", c_v, ps_ev.get(("qk", slot), 0))
                        for dc in range(8):
                            P.emit("tensor", lambda e, dc=dc, tc=tc, slot=slot, s=s, tb=tb: e.matmul(
                                PS0[:, slot:slot + 512],
                                qkwb[:, 128 * dc:128 * (dc + 1)],
                                xn2T[:, dc * NT + s * T + tb + 512 * tc: dc * NT + s * T + tb + 512 * (tc + 1)],
                                start=(dc == 0), stop=(dc == 7)),
                                inc=[(c_t, 1)] if dc == 7 else ())
                        P.wait("vector", c_t)
                        ew("vector", lambda e, rawt=rawt, tc=tc, slot=slot, bia=bia, cidx=cidx: e.tensor_scalar(
                            rawt[:, 512 * tc:512 * (tc + 1)], PS0[:, slot:slot + 512],
                            bia[:, cidx:cidx + 1], None, ALU.add), inc_c=1)
                        ps_ev[("qk", slot)] = c_v.n
                    ps_ev["qkw"] = c_t.n
                P.wait("vector", ld_sy, nct)
                P.wait("vector", c_t)
                dr()
                for (r0, r4, o0, o4) in ((rawA, rawB, at_q0, at_q4), (rawC, rawD, at_k0, at_k4)):
                    ew("vector", lambda e, r0=r0: e.tensor_tensor(rope_t1[:], r0[:], at_cos[:], ALU.mult))
                    ew("vector", lambda e, r4=r4: e.tensor_tensor(rope_t2[:], r4[:], at_sin[:], ALU.mult))
                    dr()
                    ew("vector", lambda e, o0=o0, tb=tb: e.tensor_tensor(
                        o0[:, tb:tb + 1024], rope_t1[:], rope_t2[:], ALU.subtract))
                    ew("vector", lambda e, r4=r4: e.tensor_tensor(rope_t1[:], r4[:], at_cos4[:], ALU.mult))
                    ew("vector", lambda e, r0=r0: e.tensor_tensor(rope_t2[:], r0[:], at_sin4[:], ALU.mult))
                    dr()
                    ew("vector", lambda e, o4=o4, tb=tb: e.tensor_tensor(
                        o4[:, tb:tb + 1024], rope_t1[:], rope_t2[:], ALU.add), inc_c=1)

            ps_ev["msk"] = max(ps_ev.get("msk", 0), c_v.n)
            P.wait("tensor", c_v)
            for h in (2 * pg, 2 * pg + 1, 2 * pg + 8, 2 * pg + 9):
                qb = at_q0 if h < 8 else at_q4
                kb = at_k0 if h < 8 else at_k4
                rb = 64 * (h % 2)
                P.wait("sync", c_v, ps_ev.get("msk", 0))
                dma("sync", at_cm[:], dt_in["cmul"].ap()[128 * h:128 * (h + 1), :])
                nmk = dma("sync", at_cb[:], dt_in["cbias"].ap()[128 * h:128 * (h + 1), :])
                P.wait("gpsimd", c_t, ps_ev.get("vg", 0))
                for jj in range(2):
                    srcap = Vseq.ap()[(h % 64) + 64 * jj:(h % 64) + 64 * jj + 1, :]
                    srcap = srcap.rearrange("p (tt e) -> p tt e", e=1024)[:, :, 64 * h:64 * h + 64]
                    dma("gpsimd", at_vg.ap()[jj:32:2, :], srcap, slow=True)
                nvg = ld_gp.n
                P.wait("vector", ld_sy, nmk)
                for i in range(16):
                    P.wait("tensor", c_v, ps_ev.get("sc2", 0))
                    P.emit("tensor", lambda e, qb=qb, kb=kb, rb=rb, i=i: e.matmul(
                        PS2[:, 0:128],
                        qb[rb:rb + 64, 128 * i:128 * (i + 1)],
                        kb[rb:rb + 64, 128 * i:128 * (i + 1)],
                        start=True, stop=True), inc=())
                    P.emit("tensor", lambda e, qb=qb, kb=kb, rb=rb, i=i, h=h: e.matmul(
                        PS2[:, 128:160],
                        qb[rb:rb + 64, 128 * i:128 * (i + 1)],
                        kb[rb:rb + 64, h:T:64],
                        start=False, stop=True), inc=[(c_t, 1)])
                    P.wait("vector", c_t)
                    P.wait("vector", c_s)
                    ew("vector", lambda e, i=i: e.tensor_tensor(
                        at_ms[:], PS2[:, 0:160],
                        at_cm[:, 160 * i:160 * (i + 1)], ALU.mult), inc_c=1)
                    ps_ev["sc2"] = c_v.n
                    dr()
                    ew("vector", lambda e, i=i: e.tensor_tensor(
                        at_ms[:], at_ms[:], at_cb[:, 160 * i:160 * (i + 1)], ALU.add), inc_c=1)
                    dr()
                    ew("vector", lambda e: e.tensor_reduce(
                        at_nm[:, 0:1], at_ms[:], AX.X, ALU.max, negate=True), inc_c=1)
                    P.wait("scalar", c_v)
                    ew("scalar", lambda e: e.activation(
                        at_att[:], at_ms[:], AF.Exp, bias=at_nm[:, 0:1], scale=1.0,
                        accum_out=at_rs[:, 0:1]), inc_c=1)
                    P.wait("vector", c_s)
                    ew("vector", lambda e: e.reciprocal(at_rs[:, 1:2], at_rs[:, 0:1]))
                    dr()
                    ew("vector", lambda e: e.tensor_scalar(at_at2[:], at_att[:], at_rs[:, 1:2],
                                                           None, ALU.mult), inc_c=1)
                    P.wait("tensor", c_v)
                    P.emit("tensor", lambda e: e.transpose(PT[:, 0:128], at_at2[:, 0:128], id128[:]), inc=())
                    P.emit("tensor", lambda e: e.transpose(PT[0:32, 128:256], at_at2[:, 128:160], id128[:]),
                           inc=[(c_t, 1)])
                    P.wait("vector", c_t)
                    ew("vector", lambda e: e.tensor_copy(at_dT[:], PT[:, 0:128]))
                    ew("vector", lambda e: e.tensor_copy(at_hT[:], PT[0:32, 128:256]), inc_c=1)
                    P.wait("tensor", c_v)
                    P.wait("tensor", ld_gp, nvg)
                    P.emit("tensor", lambda e, i=i, h=h: e.matmul(
                        PS1[0:64, 128 * (i % 4):128 * (i % 4) + 128],
                        Vseq[:, 1024 * i + 64 * h:1024 * i + 64 * h + 64],
                        at_dT[:], start=(i % 4 == 0), stop=False), inc=())
                    P.emit("tensor", lambda e, i=i: e.matmul(
                        PS1[0:64, 128 * (i % 4):128 * (i % 4) + 128],
                        at_vg[:], at_hT[:], start=False, stop=(i % 4 == 3)),
                        inc=[(c_t, 1)] if i % 4 == 3 else ())
                    if i % 4 == 3:
                        P.wait("vector", c_t)
                        ew("vector", lambda e, h=h, i=i: e.tensor_copy(
                            aoT[64 * (h % 2):64 * (h % 2) + 64,
                                2048 * (h // 2) + 512 * (i // 4):2048 * (h // 2) + 512 * (i // 4) + 512],
                            PS1[0:64, 0:512]), inc_c=1)
                if h == 0 and s == 0:
                    tap("h0ms", lambda: at_ms[:])
                    tap("h0att", lambda: at_att[:])
                    tap("h0nm", lambda: at_nm[:])
                    tap("h0dT", lambda: at_dT[:])
                    tap("h0vg", lambda: at_vg[:])
                    tap("h0cm", lambda: at_cm[:])
                    tap("h0ao", lambda: aoT[0:64, 0:2048])
                ps_ev["vg"] = c_t.n
                ps_ev["msk"] = c_v.n

    def wo_and_ffn_seq(s, do_ffn=True):
        for dc in range(8):
            dma("gpsimd", woTC[:, 1024 * dc:1024 * (dc + 1)],
                dt_in["woT"].ap()[128 * dc:128 * (dc + 1), :])
        nwo = ld_gp.n
        dma("sync", bor[:], dt_in["borow"].ap())
        for tt in range(16):
            dma("sync", racc[tt][:], r_scr.ap()[s * T + 128 * tt:s * T + 128 * (tt + 1), :])
        nri = ld_sy.n
        P.wait("tensor", ld_sy, nri)
        P.wait("tensor", ld_gp, nwo)
        for tt in range(16):
            slot = (0, 1024)[tt % 2]
            P.wait("tensor", c_v, ps_ev.get(("wo", slot), 0))
            for hf in range(2):
                for ec in range(8):
                    P.emit("tensor", lambda e, ec=ec, hf=hf, tt=tt, slot=slot: e.matmul(
                        PS0[:, slot + 512 * hf:slot + 512 * hf + 512],
                        aoT[:, 2048 * ec + 128 * tt:2048 * ec + 128 * (tt + 1)],
                        woTC[:, 1024 * ec + 512 * hf:1024 * ec + 512 * (hf + 1)],
                        start=(ec == 0), stop=False), inc=())
                P.emit("tensor", lambda e, hf=hf, slot=slot: e.matmul(
                    PS0[:, slot + 512 * hf:slot + 512 * hf + 512], ones[:],
                    bor[:, 512 * hf:512 * (hf + 1)], start=False, stop=True),
                    inc=[(c_t, 1)] if hf == 1 else ())
            P.wait("vector", c_t)
            P.wait("vector", ld_sy, nri)
            ew("vector", lambda e, tt=tt, slot=slot: e.tensor_tensor(
                racc[tt][:], racc[tt][:], PS0[:, slot:slot + 1024], ALU.add), inc_c=1)
            ps_ev[("wo", slot)] = c_v.n

        if not do_ffn:
            P.wait("gpsimd", c_v)
            for tt in range(16):
                dma("gpsimd", y_out.ap()[s * T + 128 * tt:s * T + 128 * (tt + 1), :], racc[tt][:])
            return
        ps_ev[("wc", 0)] = max(ps_ev.get(("wc", 0), 0), c_t.n)
        ps_ev[("wc", 1)] = max(ps_ev.get(("wc", 1), 0), c_t.n)
        ch0 = 0
        sups_l = SUPS[:1] if stop_after == "ffn1" else SUPS
        for ks, csz in enumerate(sups_l):
            buf = wcA if ks % 2 == 0 else wcB
            P.wait("gpsimd", c_t, ps_ev.get(("wc", ks % 2), 0))
            for j in range(csz):
                dma("gpsimd", buf[:, 1024 * j:1024 * (j + 1)],
                    dt_in["wcT"].ap()[128 * (ch0 + j):128 * (ch0 + j + 1), :])
            nwc = ld_gp.n
            for tp in range(1 if stop_after == "ffn1" else 8):
                tb = 256 * tp
                batches = [list(range(0, min(8, csz))), list(range(8, csz))]
                for bi, batch in enumerate(batches):
                    if not batch:
                        continue
                    P.wait("tensor", c_s)
                    P.wait("tensor", c_v)

                    for j in batch:
                        p = ch0 + j
                        rb = 32 * (p % 4)
                        off = 512 * (j % 4) + 256 * ((j % 8) // 4)
                        P.emit("tensor", lambda e, p=p, rb=rb, off=off, s=s, tb=tb: e.matmul(
                            PS0[:, off:off + 256],
                            Mpair[rb:rb + 32, 128 * (p // 4):128 * (p // 4) + 128],
                            xn2T[rb:rb + 32, ((p % 32) // 4) * NT + s * T + tb:
                                 ((p % 32) // 4) * NT + s * T + tb + 256],
                            start=True, stop=True, tile_position=(rb, 0)),
                            inc=[(c_t, 1)] if j == batch[-1] else ())
                    P.wait("scalar", c_t)
                    P.wait("scalar", c_v, ps_ev.get("eS", 0))
                    n = len(batch)
                    RSQ2 = 0.7071067811865476
                    if n == 8:
                        zsrc = [PS0[:, 0:2048].rearrange("p (b o x) -> p b x", b=4, o=2)[
                            :, :, 256 * o:256 * (o + 1)].rearrange("p b x -> p (b x)") for o in range(2)]                             if False else None
                        for o in range(2):
                            psrc = PS0[:, 0:2048].rearrange("p (b ox) -> p b ox", b=4)[
                                :, :, 256 * o:256 * (o + 1)]
                            ew("scalar", lambda e, psrc=psrc, o=o: e.activation(
                                eSb[:, 1024 * o:1024 * (o + 1)].rearrange("p (b x) -> p b x", b=4),
                                psrc, AF.Erf, bias=0.0, scale=RSQ2), inc_c=1)
                        P.wait("vector", c_s)
                        for o in range(2):
                            psrc = PS0[:, 0:2048].rearrange("p (b ox) -> p b ox", b=4)[
                                :, :, 256 * o:256 * (o + 1)]
                            gdst = Gb[:, 256 * (batch[0] + 4 * o):256 * (batch[0] + 4 * o) + 1024
                                      ].rearrange("p (b x) -> p b x", b=4)
                            ew("vector", lambda e, psrc=psrc, gdst=gdst, o=o: e.scalar_tensor_tensor(
                                gdst, eSb[:, 1024 * o:1024 * (o + 1)].rearrange("p (b x) -> p b x", b=4),
                                1.0, psrc, ALU.add, ALU.mult), inc_c=1)
                    else:
                        psrc = PS0[:, 0:2048].rearrange("p (b x) -> p b x", b=4)[:, 0:n, 0:256]
                        ew("scalar", lambda e, psrc=psrc, n=n: e.activation(
                            eSb[:, 0:256 * n].rearrange("p (b x) -> p b x", b=n),
                            psrc, AF.Erf, bias=0.0, scale=RSQ2), inc_c=1)
                        P.wait("vector", c_s)
                        ew("vector", lambda e, psrc=psrc, n=n, b0=batch[0]: e.scalar_tensor_tensor(
                            Gb[:, 256 * b0:256 * (b0 + n)].rearrange("p (b x) -> p b x", b=n),
                            eSb[:, 0:256 * n].rearrange("p (b x) -> p b x", b=n),
                            1.0, psrc, ALU.add, ALU.mult), inc_c=1)
                    ps_ev[("z", bi)] = c_v.n
                    ps_ev["eS"] = c_v.n
                P.wait("tensor", c_s)
                P.wait("tensor", ld_gp, nwc)
                for ti in range(2):
                    tt = 2 * tp + ti
                    pst = PS1[:, 0:1024] if ti == 0 else PS0[:, 0:1024]
                    P.wait("tensor", c_v, ps_ev.get(("cp", ti), 0))
                    for j in range(csz):
                        for hf in range(2):
                            P.emit("tensor", lambda e, j=j, hf=hf, ti=ti, pst=pst, buf=buf: e.matmul(
                                pst[:, 512 * hf:512 * (hf + 1)],
                                Gb[:, 256 * j + 128 * ti:256 * j + 128 * (ti + 1)],
                                buf[:, 1024 * j + 512 * hf:1024 * j + 512 * (hf + 1)],
                                start=(j == 0), stop=(j == csz - 1)),
                                inc=[(c_t, 1)] if (j == csz - 1 and hf == 1) else ())
                    P.wait("vector", c_t)
                    ew("vector", lambda e, tt=tt, pst=pst: e.scalar_tensor_tensor(
                        racc[tt][:], pst[:], 0.5, racc[tt][:], ALU.mult, ALU.add), inc_c=1)
                    ps_ev[("cp", ti)] = c_v.n
                    if ti == 1:
                        ps_ev["cpPS0"] = c_v.n
            ps_ev[("wc", ks % 2)] = c_t.n
            ch0 += csz
        if stop_after == "ffn1":
            tap("gb", lambda: Gb[:])
        P.wait("gpsimd", c_v)
        for tt in range(16):
            dma("gpsimd", y_out.ap()[s * T + 128 * tt:s * T + 128 * (tt + 1), :], racc[tt][:])

    for s in range(2):
        attn_seq(s)
        if s == 0:
            tap("aoT", lambda: aoT[:])
            tap("xn2T", lambda: xn2T[:])
            tap("atms", lambda: at_ms[:])
            tap("atatt", lambda: at_att[:])
            tap("atnm", lambda: at_nm[:])
            tap("atrs", lambda: at_rs[:])
            tap("atdT", lambda: at_dT[:])
            tap("atvg", lambda: at_vg[:])
            tap("athT", lambda: at_hT[:])
            tap("Vseq", lambda: Vseq[:])
            tap("q3", lambda: at_q0[:])
            tap("q7", lambda: at_q4[:])
            tap("k3", lambda: at_k0[:])
        P.barrier(dma_sems=[("sync", ld_sy), ("scalar", ld_sc), ("gpsimd", ld_gp)])
        if stop("ph10"):
            return dt_in, y_out, tap_list, P
        wo_and_ffn_seq(s, do_ffn=(stop_after != "wo0"))
        if stop("ffn1"):
            return dt_in, y_out, tap_list, P
        P.barrier(dma_sems=[("sync", ld_sy), ("scalar", ld_sc), ("gpsimd", ld_gp)])
        if s == 0:
            tap("yhalf", lambda: y_out.ap()[0:2048, :])
        if stop("ph11" if s == 1 else "ph11a"):
            return dt_in, y_out, tap_list, P

    P.wait("sync", ld_sy)
    P.wait("scalar", ld_sc)
    P.wait("gpsimd", ld_gp)
    P.barrier(dma_sems=[("sync", ld_sy), ("scalar", ld_sc), ("gpsimd", ld_gp)])
    return dt_in, y_out, tap_list, P


def _host_prep(inputs):
    cst = _consts()
    f32 = lambda a: np.ascontiguousarray(np.asarray(a, np.float32))
    x = f32(inputs["L0_attn_x"]).reshape(16, T, D)
    w1 = f32(inputs["L0_ffn_w1"])
    b1 = f32(inputs["L0_ffn_b1"])
    w2 = f32(inputs["L0_ffn_w2"])
    b2 = f32(inputs["L0_ffn_b2"])
    Wq = f32(inputs["L1_attn_Wq"]); bq = f32(inputs["L1_attn_bq"])
    Wk = f32(inputs["L1_attn_Wk"]); bk = f32(inputs["L1_attn_bk"])
    Wv = f32(inputs["L1_attn_Wv"]); bv = f32(inputs["L1_attn_bv"])
    Wo = f32(inputs["L1_attn_Wo"]); bo = f32(inputs["L1_attn_bo"])
    RM = f32(inputs["L1_ffn_rail_maker"])
    RU = f32(inputs["L1_ffn_rail_up_proj"])
    wc = f32(inputs["L1_ffn_w_cproj"])
    ema = f32(inputs["L0_attn_ema_out"])

    rmB = np.zeros((128, 1024), np.float32)
    ruB = np.zeros((128, 4096), np.float32)
    for p in range(128):
        hh, r0 = p // 32, 2 * (p % 32)
        rbase, cbase, kbase = 32 * (p % 4), 32 * (p // 4), 128 * (p // 4)
        for sub in range(2):
            rmB[rbase + 16 * sub:rbase + 16 * (sub + 1),
                cbase + 16 * sub:cbase + 16 * (sub + 1)] = RM[hh, r0 + sub]
            ruB[rbase + 16 * sub:rbase + 16 * (sub + 1),
                kbase + 64 * sub:kbase + 64 * (sub + 1)] = RU[hh, r0 + sub].T

    sc = np.zeros((128, 8), np.float32)
    sc[:, 0] = np.float32(inputs["L0_attn_log_tau"])
    sc[:, 1] = np.float32(inputs["L0_attn_log_blend"])
    sc[:, 2] = np.float32(inputs["L0_attn_logit_blend_seq"])
    sc[:, 3] = 0.5
    sc[:, 7] = 1.0

    shared = {
        "w1": w1, "b1c": np.ascontiguousarray(b1.reshape(32, 128).T),
        "b2f": np.ascontiguousarray(b2.reshape(8, 128).T),
        "w2": w2,
        "wqT": np.ascontiguousarray(Wq.T), "wkT": np.ascontiguousarray(Wk.T),
        "wvT": np.ascontiguousarray(Wv.T), "woT": np.ascontiguousarray(Wo.T),
        "bqf": np.ascontiguousarray(bq.reshape(8, 128).T),
        "bkf": np.ascontiguousarray(bk.reshape(8, 128).T),
        "bvrow": np.ascontiguousarray(bv[None, :]),
        "borow": np.ascontiguousarray(bo[None, :]),
        "wcT": np.ascontiguousarray(wc.T),
        "rmB": rmB.astype(BFNP), "ruB": ruB.astype(BFNP),
        "cmul": cst["cmul"], "cbias": cst["cbias"],
        "l0diag": cst["l0diag"],
        "cosT": cst["cosT"], "sinT": cst["sinT"],
        "id128": cst["id128"], "ones_row": cst["ones_row"],
        "emaf": np.ascontiguousarray(ema.reshape(8, 128).T),
        "sca": sc,
    }
    in_maps = []
    for c in range(NCORES):
        m = dict(shared)
        m["x"] = np.ascontiguousarray(x[2 * c:2 * c + 2].reshape(NT, D))
        in_maps.append(m)
    return in_maps


def kernel(**inputs):
    nc = bass.Bass("TRN2", target_bir_lowering=False)
    dt_in, y_out, tap_list, P = build(nc)
    in_maps = _host_prep(inputs)
    _realize(nc, P)
    res = run_bass_kernel_spmd(nc, in_maps, core_ids=list(range(NCORES)))
    out = np.stack([r["y"] for r in res.results], axis=0)
    return out.reshape(16, T, D).astype(np.float32)


def _realize(nc, P):
    from contextlib import ExitStack
    es = ExitStack()
    for s in P.sems:
        s.h = es.enter_context(nc.semaphore(s.name))
    block = es.enter_context(nc.Block())
    with es:
        for ename in P.ENGS:
            ops = P.q[ename]
            getattr(block, ename)(lambda eng, ops=ops: [op(eng) for op in ops])
